# version 01
# baseline (speedup 1.0000x reference)
import contextlib

import numpy as np
import concourse.bass as bass
import concourse.mybir as mybir
from concourse.bass_utils import run_bass_kernel_spmd

F32 = mybir.dt.float32
BF16 = mybir.dt.bfloat16
AF = mybir.ActivationFunctionType
ALU = mybir.AluOpType
AX = mybir.AxisListType

MAGIC = float(np.float32(1.5 * 2**23))

D = 1024
H = 4
DK = 256
DV = 256
DFF = 3072
CH = 32
GRP = 128
QMAX = 32767.0

ENGS = ("t", "v", "a", "p", "s")
DMAQS = ("ds", "da", "dp")
_Q2E = {"ds": "s", "da": "a", "dp": "p"}


class Buf:
    __slots__ = ("name", "w", "r")

    def __init__(self, name):
        self.name = name
        self.w = None
        self.r = []


class Prog:

    def __init__(self, nc):
        self.nc = nc
        self.stream = {e: [] for e in ENGS}
        self.cnt = {k: 0 for k in ENGS + DMAQS}
        self.waited = {e: {} for e in ENGS}
        self.bufs = {}

    def B(self, name):
        if name not in self.bufs:
            self.bufs[name] = Buf(name)
        return self.bufs[name]

    def _wait(self, eng, tok):
        if tok is None:
            return
        pk, pv = tok
        if self.waited[eng].get(pk, -1) >= pv:
            return
        self.waited[eng][pk] = pv
        self.stream[eng].append(("w", pk, pv))

    def _deps(self, eng, reads, writes):
        for b in reads or ():
            self._wait(eng, b.w)
        for b in writes or ():
            self._wait(eng, b.w)
            for t in b.r:
                self._wait(eng, t)

    def _note(self, tok, reads, writes):
        for b in writes or ():
            b.w = tok
            b.r = []
        for b in reads or ():
            b.r.append(tok)

    def op(self, eng, fn, reads=None, writes=None):
        assert eng in ("t", "v", "a", "p")
        self._deps(eng, reads, writes)
        self.cnt[eng] += 1
        tok = (eng, self.cnt[eng])
        self.stream[eng].append(("i", fn))
        self._note(tok, reads, writes)
        return tok

    def dma(self, q, out, in_, reads=None, writes=None, **kw):
        eng = _Q2E[q]
        self._deps(eng, reads, writes)
        self.cnt[q] += 16
        tok = (q, self.cnt[q])
        self.stream[eng].append(("d", q, out, in_, kw))
        self._note(tok, reads, writes)
        return tok

    def wait(self, eng, tok):
        self._wait(eng, tok)

    def qtok(self, q):
        return (q, self.cnt[q]) if self.cnt[q] else None

    def barrier(self):
        for k in ("t", "a", "p", "s", "ds", "da", "dp"):
            if self.cnt[k]:
                self._wait("v", (k, self.cnt[k]))
        self.cnt["v"] += 1
        self.stream["v"].append(("i", lambda v: v.engine_nop()))
        tok = ("v", self.cnt["v"])
        for e in ("t", "a", "p", "s"):
            self._wait(e, tok)
        for b in self.bufs.values():
            b.w = tok
            b.r = []
        return tok

    def build(self, es):
        nc = self.nc
        sems = {k: es.enter_context(nc.semaphore("sem_" + k)) for k in ENGS + DMAQS}
        block = es.enter_context(nc.Block())

        def run_stream(engobj, key):
            qs = [q for q, e in _Q2E.items() if e == key]
            for item in self.stream[key]:
                if item[0] == "w":
                    engobj.wait_ge(sems[item[1]], item[2])
                elif item[0] == "i":
                    item[1](engobj).then_inc(sems[key], 1)
                else:
                    _, q, out, in_, kw = item
                    engobj.dma_start(out=out, in_=in_, **kw).then_inc(sems[q], 16)

        @block.tensor
        def _(t):
            run_stream(t, "t")

        @block.vector
        def _(v):
            run_stream(v, "v")

        @block.scalar
        def _(a):
            run_stream(a, "a")

        @block.gpsimd
        def _(g):
            run_stream(g, "p")

        @block.sync
        def _(s):
            run_stream(s, "s")


def host_constants():
    i = np.arange(GRP)
    blk = i // CH
    row, col = np.meshgrid(i, i, indexing="ij")
    brow, bcol = blk[row], blk[col]
    bdSLneg = -((brow == bcol) & (row > col)).astype(np.float32)
    aboveneg = -(brow < bcol).astype(np.float32)
    masku = (row <= col).astype(np.float32)
    eye = np.eye(GRP, dtype=np.float32)
    idn = np.eye(32, dtype=np.float32)
    onesrow = np.ones((1, 128), dtype=np.float32)
    hsel = np.zeros((128, 16), dtype=np.float32)
    for h in range(H):
        hsel[:, 4 * h + h] = 1.0
    return {
        "c_bdSLneg": bdSLneg, "c_aboveneg": aboveneg, "c_maskU": masku,
        "c_eye": eye, "c_idn": idn, "c_onesrow": onesrow, "c_hsel": hsel,
    }


def build_core_program(Lc, TT, with_attn=True, dbg=False):
    NT = Lc // TT
    NG = TT // GRP
    nc = bass.Bass("TRN2", target_bir_lowering=False)
    es = contextlib.ExitStack()
    p = Prog(nc)
    B = p.B

    def din(name, shape):
        return nc.dram_tensor(name, shape, F32, kind="ExternalInput")

    x_h = din("x", [Lc, D])
    w_attn = {n: din(n, [D, D]) for n in ("Wq", "Wk", "Wv", "Wo")}
    conv_h = {n: din(n, [D, 4]) for n in ("conv_q", "conv_k", "conv_v")}
    wb_h, wg_h, wh_h = din("Wb", [D, 2 * H]), din("Wg", [D, H]), din("Wh", [D, H])
    bg_h, bh_h = din("bg", [H]), din("bh", [H])
    nw_h = din("norm_w", [DV])
    wup_h, wgt_h = din("w_up", [DFF, D]), din("w_gate", [DFF, D])
    wdn_h = din("w_down", [D, DFF])
    consts = {k: din(k, list(v.shape)) for k, v in host_constants().items()}

    out_h = nc.dram_tensor("out", [Lc, D], F32, kind="ExternalOutput")

    def dscr(name, shape, dt=BF16):
        return nc.dram_tensor(name, shape, dt, kind="Internal")

    xn_d = dscr("xn_d", [Lc, D])
    xq_d = dscr("xq_d", [D, Lc])
    up_d = dscr("up_d", [DFF, Lc])
    gt_d = dscr("gt_d", [DFF, Lc])
    gi2_d = dscr("gi2_d", [DFF, Lc])
    oi_d = dscr("oi_d", [DFF, Lc])
    s_d = dscr("s_d", [D, Lc])
    wdq_d = {"up": dscr("wdq_up_d", [DFF, D]), "gt": dscr("wdq_gt_d", [DFF, D]),
             "dn": dscr("wdq_dn_d", [D, DFF])}
    kb_d = dscr("kb_d", [D, TT])
    vb_d = dscr("vb_d", [D, TT])
    ot_d = dscr("ot_d", [TT, D])
    gf_d = dscr("gf_d", [D, TT])
    gmx_d = dscr("gmx_d", [128, 1], F32)
    cc_in = [dscr("cc_in%d" % i, [1, 1], F32) for i in range(5)]
    cc_out = [dscr("cc_out%d" % i, [1, 1], F32) for i in range(5)]

    def sb(name, shape, dt):
        return es.enter_context(nc.sbuf_tensor(name, shape, dt))

    mskD = sb("mskD", [128, 128], BF16)
    mskL = sb("mskL", [128, 128], BF16)
    msku = sb("msku", [128, 128], BF16)
    eye_s = sb("eye_s", [128, 128], BF16)
    idn_s = sb("idn_s", [32, 32], F32)
    onesr = sb("onesr", [1, 128], F32)
    hsel_s = sb("hsel_s", [128, 16], BF16)
    cw_s = sb("cw_s", [128, 96], F32)
    nwbc = sb("nwbc", [128, 256], F32)
    nwrow = sb("nwrow", [1, 256], F32)
    biasc = sb("biasc", [32, 1], F32)
    epsc = sb("epsc", [128, 4], F32)
    scal = sb("scal", [1, 32], F32)
    scalb = sb("scalb", [128, 16], F32)

    Sf = sb("Sf", [128, 4096], F32)
    Sbf = sb("Sbf", [128, 4096], BF16)

    warena = sb("warena", [128, 24576], BF16)
    woarena = sb("woarena", [128, 8192], BF16)
    xstage = sb("xstage", [128, 3072], F32)
    xnfm = sb("xnfm", [128, 4096], BF16)
    bstg1 = sb("bstg1", [128, 4096], BF16)
    bstg2 = sb("bstg2", [128, 4096], BF16)
    bstg3 = sb("bstg3", [128, 4096], BF16)
    fstg1 = sb("fstg1", [128, 3072], F32)
    prear = sb("prear", [128, 12384], BF16)
    qkvfm = sb("qkvfm", [128, 8192], BF16)
    chain = sb("chain", [128, 2560], BF16)
    solv = sb("solv", [128, 2048], BF16)
    rtmp = sb("rtmp", [128, 512], F32)
    colsA = sb("colsA", [128, 256], F32)
    strips = sb("strips", [32, 512], F32)
    strips2 = sb("strips2", [4, 1024], F32)
    wsm = sb("wsm", [128, 128], BF16)
    striptm = sb("striptm", [128, 128], F32)
    mcol = sb("mcol", [128, 8], F32)
    mstash = sb("mstash", [128, 96], BF16)

    ps = [es.enter_context(nc.psum_tensor("psb%d" % i, [128, 512], F32)) for i in range(8)]
    PSA, PSB, PSK, PSR, PSW, PSO, PSX, PSS = ps

    def mm(out_ap, lhsT, rhs, start, stop, reads, writes, tp=None):
        return p.op(
            "t",
            lambda t, o=out_ap, l=lhsT, r=rhs, st=start, sp=stop, q=tp: t.matmul(
                o, l, r, start=st, stop=sp, tile_position=q),
            reads=reads, writes=writes)

    def pick(cands=("v", "p")):
        return min(cands, key=lambda e: p.cnt[e])

    def ts(eng, out_ap, in_ap, s1, s2, op0, op1, reads, writes):
        if eng == "p" and not isinstance(s1, (int, float)):
            eng = "v"
        if op1 is None:
            return p.op(eng, lambda e, o=out_ap, i=in_ap, a=s1, o0=op0:
                        e.tensor_scalar(o, i, a, None, o0), reads=reads, writes=writes)
        return p.op(eng, lambda e, o=out_ap, i=in_ap, a=s1, b=s2, o0=op0, o1=op1:
                    e.tensor_scalar(o, i, a, b, o0, o1), reads=reads, writes=writes)

    def tt(eng, out_ap, a_ap, b_ap, alu, reads, writes):
        return p.op(eng, lambda e, o=out_ap, x=a_ap, y=b_ap, u=alu:
                    e.tensor_tensor(o, x, y, u), reads=reads, writes=writes)

    def stt(eng, out_ap, in0, s1, in1, op0, op1, reads, writes):
        if eng == "p" and not isinstance(s1, (int, float)):
            eng = "v"
        return p.op(eng, lambda e, o=out_ap, i=in0, a=s1, j=in1, o0=op0, o1=op1:
                    e.scalar_tensor_tensor(o, i, a, j, o0, o1), reads=reads, writes=writes)

    def act(out_ap, in_ap, func, reads, writes, bias=0.0, scale=1.0, accum=None):
        return p.op("a", lambda a, o=out_ap, i=in_ap, f=func, bi=bias, sc=scale, au=accum:
                    a.activation(o, i, f, bias=bi, scale=sc, accum_out=au),
                    reads=reads, writes=writes)

    def vtrans(out_ap, in_ap, reads, writes):
        return p.op("v", lambda v, o=out_ap, i=in_ap: v.transpose(o, i),
                    reads=reads, writes=writes)

    def copy(eng, out_ap, in_ap, reads, writes):
        return p.op(eng, lambda e, o=out_ap, i=in_ap: e.tensor_copy(o, i),
                    reads=reads, writes=writes)

    def memset(eng, ap, val, writes):
        return p.op(eng, lambda e, a=ap, v=val: e.memset(a, v), writes=writes)

    def reduce(eng, out_ap, in_ap, alu, reads, writes, absval=False):
        return p.op(eng, lambda e, o=out_ap, i=in_ap, u=alu, av=absval:
                    e.tensor_reduce(o, i, AX.X, u, apply_absolute_value=av),
                    reads=reads, writes=writes)

    for name, sbuf, w in (("c_bdSLneg", mskD, 128), ("c_aboveneg", mskL, 128),
                          ("c_maskU", msku, 128), ("c_eye", eye_s, 128),
                          ("c_hsel", hsel_s, 16)):
        p.dma("ds", xstage[:, 0:w], consts[name].ap(), reads=[B("xstage")],
              writes=[B("xstage")])
        copy("v", sbuf[:, 0:w], xstage[:, 0:w],
             reads=[B("xstage")], writes=[B(sbuf.name)])
    p.dma("ds", idn_s[:], consts["c_idn"].ap(), writes=[B("idn_s")])
    p.dma("ds", onesr[:], consts["c_onesrow"].ap(), writes=[B("onesr")])
    for j, hname in enumerate(("conv_q", "conv_k", "conv_v")):
        p.dma("ds", cw_s[:, 32 * j:32 * (j + 1)],
              bass.AP(conv_h[hname], 0, [[4, 128], [512, 8], [1, 4]]),
              reads=[B("cw_s")], writes=[B("cw_s")])
    memset("v", biasc[:], 0.0, writes=[B("biasc")])
    p.dma("ds", biasc[8:12, 0:1], bass.AP(bg_h, 0, [[1, 4], [1, 1]]),
          reads=[B("biasc")], writes=[B("biasc")])
    p.dma("ds", biasc[12:16, 0:1], bass.AP(bh_h, 0, [[1, 4], [1, 1]]),
          reads=[B("biasc")], writes=[B("biasc")])
    memset("v", epsc[:, 0:1], 1e-5, writes=[B("epsc")])
    memset("v", epsc[:, 1:2], 1e-12, writes=[B("epsc")])
    p.dma("ds", nwrow[:], nw_h.ap().rearrange("(a d) -> a d", a=1), writes=[B("nwrow")])
    mm(PSO[:, 256:512], onesr[:], nwrow[:], True, True,
       [B("onesr"), B("nwrow")], [B("psO2")])
    copy("v", nwbc[:], PSO[:, 256:512], reads=[B("psO2")], writes=[B("nwbc")])
    memset("v", Sf[:], 0.0, writes=[B("Sf")])
    memset("p", Sbf[:], 0.0, writes=[B("Sbf")])
    memset("v", mcol[:], 0.0, writes=[B("mcol")])
    memset("v", mstash[:], 0.0, writes=[B("mstash")])

    def dequant(w_h, w_d, O, I):
        nblk = I // 32
        for r in range(O // 128):
            stg = B("xstage")
            p.dma("ds", xstage[:, 0:I], w_h.ap()[128 * r:128 * (r + 1), :],
                  reads=[stg], writes=[stg])
            reduce("v", rtmp[:, 0:nblk],
                   xstage[:, 0:I].rearrange("q (b j) -> q b j", j=32), ALU.max,
                   reads=[stg], writes=[B("dq_bm")], absval=True)
            ts("v", rtmp[:, 128:128 + nblk], rtmp[:, 0:nblk], 1.0 / 7.0, 1e-12,
               ALU.mult, ALU.max, reads=[B("dq_bm")], writes=[B("dq_s")])
            p.op("v", lambda v, n=nblk: v.reciprocal(rtmp[:, 256:256 + n], rtmp[:, 128:128 + n]),
                 reads=[B("dq_s")], writes=[B("dq_rs")])
            rs_b = bass.AP(rtmp, 256, [[512, 128], [1, nblk], [0, 32]])
            sw_b = bass.AP(rtmp, 128, [[512, 128], [1, nblk], [0, 32]])
            tt(pick(), fstg1[:, 0:I].rearrange("q (b j) -> q b j", j=32),
               xstage[:, 0:I].rearrange("q (b j) -> q b j", j=32), rs_b, ALU.mult,
               reads=[stg, B("dq_rs")], writes=[B("dq_t")])
            ts(pick(), fstg1[:, 0:I], fstg1[:, 0:I], MAGIC, MAGIC, ALU.add, ALU.subtract,
               reads=[B("dq_t")], writes=[B("dq_t")])
            tt(pick(), bstg1[:, 0:I].rearrange("q (b j) -> q b j", j=32),
               fstg1[:, 0:I].rearrange("q (b j) -> q b j", j=32), sw_b, ALU.mult,
               reads=[B("dq_t"), B("dq_s")], writes=[B("dq_o")])
            p.dma("dp", w_d.ap()[128 * r:128 * (r + 1), :], bstg1[:, 0:I], reads=[B("dq_o")])

    dequant(wup_h, wdq_d["up"], DFF, D)
    dequant(wgt_h, wdq_d["gt"], DFF, D)
    dequant(wdn_h, wdq_d["dn"], D, DFF)
    p.barrier()

    for i in range(Lc // 128):
        j = i % 2
        xs = B("xst%d" % j)
        sl = xstage[:, 1024 * j:1024 * j + 1024]
        p.dma("ds", sl, x_h.ap()[128 * i:128 * (i + 1), :], writes=[xs])
        mb, sq, vb, rb = B("p0m%d" % j), B("p0sq%d" % j), B("p0v%d" % j), B("p0r%d" % j)
        c = 16 * j
        reduce("v", rtmp[:, c:c + 1], sl, ALU.add, reads=[xs], writes=[mb])
        ts("v", rtmp[:, c + 1:c + 2], rtmp[:, c:c + 1], 1.0 / D, None, ALU.mult, None,
           reads=[mb], writes=[mb])
        act(fstg1[:, 1024 * j:1024 * j + 1024], sl, AF.Square,
            reads=[xs], writes=[sq, B("fst0%d" % j)], accum=rtmp[:, c + 2:c + 3])
        tt("v", rtmp[:, c + 3:c + 4], rtmp[:, c + 1:c + 2], rtmp[:, c + 1:c + 2], ALU.mult,
           reads=[mb], writes=[vb])
        ts("v", rtmp[:, c + 4:c + 5], rtmp[:, c + 2:c + 3], 1.0 / D, None, ALU.mult, None,
           reads=[sq], writes=[vb])
        tt("v", rtmp[:, c + 5:c + 6], rtmp[:, c + 4:c + 5], rtmp[:, c + 3:c + 4], ALU.subtract,
           reads=[vb], writes=[vb])
        act(rtmp[:, c + 6:c + 7], rtmp[:, c + 5:c + 6], AF.Sqrt,
            reads=[vb, B("epsc")], writes=[rb], bias=epsc[:, 0:1])
        p.op("v", lambda v, c=c: v.reciprocal(rtmp[:, c + 7:c + 8], rtmp[:, c + 6:c + 7]),
             reads=[rb], writes=[rb])
        xn = B("xnt%d" % j)
        ts("v", bstg1[:, 1024 * j:1024 * j + 1024], sl, rtmp[:, c + 1:c + 2],
           rtmp[:, c + 7:c + 8], ALU.subtract, ALU.mult, reads=[xs, mb, rb], writes=[xn])
        reduce("v", rtmp[:, c + 8:c + 9], bstg1[:, 1024 * j:1024 * j + 1024], ALU.max,
               reads=[xn], writes=[B("p0mx%d" % j)], absval=True)
        tt("v", mcol[:, 0:1], mcol[:, 0:1], rtmp[:, c + 8:c + 9], ALU.max,
           reads=[B("p0mx%d" % j)], writes=[B("mcol")])
        p.dma("dp", xn_d.ap()[128 * i:128 * (i + 1), :], bstg1[:, 1024 * j:1024 * j + 1024],
              reads=[xn])

    def gmax_sync(idx, slot, dst):
        p.dma("dp", gmx_d.ap(), mcol[:, slot:slot + 1], reads=[B("mcol")], writes=[B("gmx")])
        p.dma("ds", xstage[0:1, 0:128], gmx_d.ap().rearrange("q one -> one q"),
              reads=[B("gmx"), B("xstage")], writes=[B("xstage")])
        reduce("v", scal[0:1, dst:dst + 1], xstage[0:1, 0:128], ALU.max,
               reads=[B("xstage")], writes=[B("scal")])
        p.dma("dp", cc_in[idx].ap(), scal[0:1, dst:dst + 1],
              reads=[B("scal")], writes=[B("cci%d" % idx)])
        p.op("p", lambda g, i=idx: g.collective_compute(
            "AllReduce", ALU.max, replica_groups=[list(range(8))],
            ins=[cc_in[i].ap()], outs=[cc_out[i].ap()]),
            reads=[B("cci%d" % idx)], writes=[B("cco%d" % idx)])
        p.dma("ds", scal[0:1, dst:dst + 1], cc_out[idx].ap(),
              reads=[B("cco%d" % idx)], writes=[B("scal")])

    def scal_op(fn):
        return p.op("v", fn, reads=[B("scal")], writes=[B("scal")])

    def bcast_scal(srcoff, n, dstoff):
        mm(PSO[:, 256:256 + n], onesr[:], scal[0:1, srcoff:srcoff + n], True, True,
           [B("onesr"), B("scal")], [B("psO2")])
        copy("v", scalb[:, dstoff:dstoff + n], PSO[:, 256:256 + n],
             reads=[B("psO2")], writes=[B("scalb")])

    gmax_sync(0, 0, 0)
    scal_op(lambda v: v.tensor_scalar(scal[0:1, 1:2], scal[0:1, 0:1], 1.0 / QMAX, 1e-12,
                                      ALU.mult, ALU.max))
    scal_op(lambda v: v.reciprocal(scal[0:1, 2:3], scal[0:1, 1:2]))
    bcast_scal(1, 2, 0)

    def load_wT(key, O, I):
        for k in range(I // 128):
            p.dma("da", warena[:, O * k:O * (k + 1)], wdq_d[key].ap()[:, 128 * k:128 * (k + 1)],
                  transpose=True, writes=[B("warena")])

    def emit_pass_b(key, dst_d, mslot, do_xq):
        load_wT(key, DFF, D)
        for it in range(NT):
            t0 = it * TT
            if do_xq:
                for k in range(8):
                    p.dma("da", xnfm[:, TT * k:TT * (k + 1)],
                          xn_d.ap()[t0:t0 + TT, 128 * k:128 * (k + 1)], transpose=True,
                          reads=[B("xnfm")], writes=[B("xnfm")])
                for k in range(8):
                    f = B("bqf%d" % (k % 4))
                    ts(pick(), fstg1[:, 512 * (k % 4):512 * (k % 4) + TT],
                       xnfm[:, TT * k:TT * (k + 1)], scalb[:, 1:2], MAGIC, ALU.mult, ALU.add,
                       reads=[B("xnfm"), B("scalb")], writes=[f])
                    ts(pick(), bstg1[:, TT * k:TT * (k + 1)],
                       fstg1[:, 512 * (k % 4):512 * (k % 4) + TT], MAGIC, None,
                       ALU.subtract, None, reads=[f], writes=[B("xqb")])
                p.dma("dp", bass.AP(xq_d, t0, [[Lc, 128], [128 * Lc, 8], [1, TT]]),
                      bstg1[:, 0:8 * TT], reads=[B("xqb")])
            else:
                p.dma("ds", bstg1[:, 0:8 * TT],
                      bass.AP(xq_d, t0, [[Lc, 128], [128 * Lc, 8], [1, TT]]),
                      reads=[B("xqb")], writes=[B("xqb")])
            for m in range(DFF // 128):
                bank = PSA if (m % 2 == 0) else PSB
                bb = B("psAB%d" % (m % 2))
                for k in range(8):
                    mm(bank[:, 0:TT], warena[:, DFF * k + 128 * m:DFF * k + 128 * (m + 1)],
                       bstg1[:, TT * k:TT * (k + 1)], k == 0, k == 7,
                       [B("warena"), B("xqb")], [bb])
                ev = B("bev%d" % (m % 4))
                act(bstg2[:, 512 * (m % 4):512 * (m % 4) + TT], bank[:, 0:TT], AF.Copy,
                    reads=[bb, B("scalb")], writes=[ev], scale=scalb[:, 0:1])
                reduce("v", rtmp[:, 20 + (m % 4):21 + (m % 4)],
                       bstg2[:, 512 * (m % 4):512 * (m % 4) + TT], ALU.max,
                       reads=[ev], writes=[B("bmx%d" % (m % 4))], absval=True)
                tt("v", mcol[:, mslot:mslot + 1], mcol[:, mslot:mslot + 1],
                   rtmp[:, 20 + (m % 4):21 + (m % 4)], ALU.max,
                   reads=[B("bmx%d" % (m % 4))], writes=[B("mcol")])
                p.dma("dp", dst_d.ap()[128 * m:128 * (m + 1), t0:t0 + TT],
                      bstg2[:, 512 * (m % 4):512 * (m % 4) + TT], reads=[ev])

    emit_pass_b("up", up_d, 1, True)
    p.barrier()
    emit_pass_b("gt", gt_d, 2, False)
    p.barrier()

    gmax_sync(1, 1, 3)
    gmax_sync(2, 2, 5)
    for c in (3, 5):
        scal_op(lambda v, c=c: v.tensor_scalar(scal[0:1, c:c + 1], scal[0:1, c:c + 1],
                                               1.0 / QMAX, 1e-12, ALU.mult, ALU.max))
        scal_op(lambda v, c=c: v.reciprocal(scal[0:1, c + 1:c + 2], scal[0:1, c:c + 1]))
    bcast_scal(3, 4, 2)

    for it in range(NT):
        t0 = it * TT
        for m in range(DFF // 128):
            j = m % 2
            gb = gi = B("m_b1_%d" % j)
            f1 = B("m_f1_%d" % j)
            gq = B("m_b2_%d" % j)
            sg = B("m_b3_%d" % j)
            sl1 = bstg1[:, 512 * j:512 * j + TT]
            sl2 = bstg2[:, 512 * j:512 * j + TT]
            sl3 = bstg3[:, 512 * j:512 * j + TT]
            slf = fstg1[:, 512 * j:512 * j + TT]
            p.dma("ds", sl1, gt_d.ap()[128 * m:128 * (m + 1), t0:t0 + TT],
                  reads=[gb], writes=[gb])
            ts(pick(), slf, sl1, scalb[:, 5:6], MAGIC, ALU.mult, ALU.add,
               reads=[gb, B("scalb")], writes=[f1])
            ts(pick(), sl2, slf, MAGIC, None, ALU.subtract, None, reads=[f1], writes=[gq])
            act(slf, sl2, AF.Sigmoid, reads=[gq, B("scalb")], writes=[f1], scale=scalb[:, 4:5])
            ts(pick(), slf, slf, 65536.0, MAGIC, ALU.mult, ALU.add, reads=[f1], writes=[f1])
            ts(pick(), sl3, slf, MAGIC, 65535.0, ALU.subtract, ALU.min, reads=[f1], writes=[sg])
            tt(pick(), sl1, sl2, sl3, ALU.mult, reads=[gq, sg], writes=[gi])
            reduce("v", rtmp[:, 24 + j:25 + j], sl1, ALU.max, reads=[gi],
                   writes=[B("cmx%d" % j)], absval=True)
            tt("v", mcol[:, 3:4], mcol[:, 3:4], rtmp[:, 24 + j:25 + j], ALU.max,
               reads=[B("cmx%d" % j)], writes=[B("mcol")])
            p.dma("dp", gi2_d.ap()[128 * m:128 * (m + 1), t0:t0 + TT], sl1, reads=[gi])
    p.barrier()

    gmax_sync(3, 3, 7)
    scal_op(lambda v: v.tensor_tensor(scal[0:1, 7:8], scal[0:1, 7:8], scal[0:1, 5:6], ALU.mult))
    scal_op(lambda v: v.tensor_scalar(scal[0:1, 7:8], scal[0:1, 7:8], 1.0 / 65536.0, None,
                                      ALU.mult))
    scal_op(lambda v: v.tensor_scalar(scal[0:1, 8:9], scal[0:1, 7:8], 1.0 / QMAX, 1e-12,
                                      ALU.mult, ALU.max))
    scal_op(lambda v: v.reciprocal(scal[0:1, 9:10], scal[0:1, 8:9]))
    scal_op(lambda v: v.tensor_tensor(scal[0:1, 10:11], scal[0:1, 5:6], scal[0:1, 9:10], ALU.mult))
    scal_op(lambda v: v.tensor_scalar(scal[0:1, 10:11], scal[0:1, 10:11], 1.0 / 65536.0, None,
                                      ALU.mult))
    bcast_scal(8, 3, 6)

    for it in range(NT):
        t0 = it * TT
        for m in range(DFF // 128):
            j = m % 2
            gb = oi = B("m_b1_%d" % j)
            ub = uq = B("m_b2_%d" % j)
            f1 = B("m_f1_%d" % j)
            f2 = B("m_f1h_%d" % j)
            g2 = B("m_b3_%d" % j)
            sl1 = bstg1[:, 512 * j:512 * j + TT]
            sl2 = bstg2[:, 512 * j:512 * j + TT]
            sl3 = bstg3[:, 512 * j:512 * j + TT]
            slf = fstg1[:, 512 * j:512 * j + TT]
            slf2 = fstg1[:, 512 * (2 + j):512 * (2 + j) + TT]
            p.dma("ds", sl1, gi2_d.ap()[128 * m:128 * (m + 1), t0:t0 + TT],
                  reads=[gb], writes=[gb])
            p.dma("ds", sl2, up_d.ap()[128 * m:128 * (m + 1), t0:t0 + TT],
                  reads=[ub], writes=[ub])
            ts(pick(), slf, sl1, scalb[:, 8:9], MAGIC, ALU.mult, ALU.add,
               reads=[gb, B("scalb")], writes=[f1])
            ts(pick(), sl3, slf, MAGIC, None, ALU.subtract, None, reads=[f1], writes=[g2])
            ts(pick(), slf2, sl2, scalb[:, 3:4], MAGIC, ALU.mult, ALU.add,
               reads=[ub, B("scalb")], writes=[f2])
            ts(pick(), sl2, slf2, MAGIC, None, ALU.subtract, None, reads=[f2], writes=[uq])
            tt(pick(), sl1, sl3, sl2, ALU.mult, reads=[g2, uq], writes=[oi])
            reduce("v", rtmp[:, 26 + j:27 + j], sl1, ALU.max, reads=[oi],
                   writes=[B("dmx%d" % j)], absval=True)
            tt("v", mcol[:, 4:5], mcol[:, 4:5], rtmp[:, 26 + j:27 + j], ALU.max,
               reads=[B("dmx%d" % j)], writes=[B("mcol")])
            p.dma("dp", oi_d.ap()[128 * m:128 * (m + 1), t0:t0 + TT], sl1, reads=[oi])
    p.barrier()

    gmax_sync(4, 4, 11)
    scal_op(lambda v: v.tensor_tensor(scal[0:1, 13:14], scal[0:1, 2 + 1:4], scal[0:1, 8:9],
                                      ALU.mult))
    scal_op(lambda v: v.tensor_tensor(scal[0:1, 11:12], scal[0:1, 11:12], scal[0:1, 13:14],
                                      ALU.mult))
    scal_op(lambda v: v.tensor_scalar(scal[0:1, 12:13], scal[0:1, 11:12], 1.0 / QMAX, 1e-12,
                                      ALU.mult, ALU.max))
    scal_op(lambda v: v.reciprocal(scal[0:1, 14:15], scal[0:1, 12:13]))
    scal_op(lambda v: v.tensor_tensor(scal[0:1, 14:15], scal[0:1, 13:14], scal[0:1, 14:15],
                                      ALU.mult))
    bcast_scal(12, 3, 9)

    if with_attn:
        _emit_pass_a(**{k: v for k, v in locals().items() if k in _PASS_A_ARGS})
    else:
        zl = B("zl")
        CW = min(2048, Lc)
        memset("v", bstg1[:, 0:CW], 1.0, writes=[zl])
        for r in range(D // 128):
            for c in range(Lc // CW):
                p.dma("dp", s_d.ap()[128 * r:128 * (r + 1), CW * c:CW * (c + 1)],
                      bstg1[:, 0:CW], reads=[zl])
    p.barrier()

    load_wT("dn", D, DFF)
    for it in range(NT):
        t0 = it * TT
        oqslots = []
        for m in range(DFF // 128):
            j = m % 2
            tgt = (bstg1, bstg2, bstg3)[m // 8]
            sl = tgt[:, 512 * (m % 8):512 * (m % 8) + TT]
            ob = oq = B("m_oq_%d" % m)
            f1 = B("m_f1e_%d" % j)
            slf = fstg1[:, 2048 + 512 * j:2048 + 512 * j + TT]
            p.dma("ds", sl, oi_d.ap()[128 * m:128 * (m + 1), t0:t0 + TT],
                  writes=[ob])
            ts(pick(), slf, sl, scalb[:, 11:12], MAGIC, ALU.mult, ALU.add,
               reads=[ob, B("scalb")], writes=[f1])
            ts(pick(), sl, slf, MAGIC, None, ALU.subtract, None, reads=[f1], writes=[oq])
            oqslots.append((sl, oq))
        for m in range(D // 128):
            bank = PSA if (m % 2 == 0) else PSB
            bb = B("psAB%d" % (m % 2))
            for k in range(DFF // 128):
                sl, oq = oqslots[k]
                mm(bank[:, 0:TT], warena[:, D * k + 128 * m:D * k + 128 * (m + 1)],
                   sl, k == 0, k == DFF // 128 - 1, [B("warena"), B("m_oq_%d" % k)], [bb])
            gv = B("egv%d" % (m % 4))
            act(xnfm[:, 512 * (m % 4):512 * (m % 4) + TT], bank[:, 0:TT], AF.Silu,
                reads=[bb, B("scalb")], writes=[gv], scale=scalb[:, 9:10])
            p.dma("dp", gf_d.ap()[128 * m:128 * (m + 1), 0:TT],
                  xnfm[:, 512 * (m % 4):512 * (m % 4) + TT], reads=[gv],
                  writes=[B("d_gf%d" % m)])
        for g in range(NG):
            j = g % 2
            gt, st, xr = B("egt%d" % j), B("est%d" % j), B("exr%d" % j)
            cb = B("ecb%d" % j)
            gsl = prear[:, 1024 * j:1024 * j + 1024]
            ssl = prear[:, 4096 + 1024 * j:4096 + 1024 * j + 1024]
            xsl = xstage[:, 1024 * j:1024 * j + 1024]
            fsl = fstg1[:, 1024 * j:1024 * j + 1024]
            p.dma("da", gsl, gf_d.ap()[:, GRP * g:GRP * (g + 1)], transpose=True,
                  reads=[gt] + [B("d_gf%d" % m) for m in range(8)], writes=[gt])
            p.dma("da", ssl, s_d.ap()[:, t0 + GRP * g:t0 + GRP * (g + 1)], transpose=True,
                  reads=[st], writes=[st])
            p.dma("ds", xsl, x_h.ap()[t0 + GRP * g:t0 + GRP * (g + 1), :],
                  reads=[xr], writes=[xr])
            tt("v", fsl, gsl, ssl, ALU.mult, reads=[gt, st], writes=[cb])
            tt("v", fsl, fsl, xsl, ALU.add, reads=[cb, xr], writes=[cb])
            p.dma("dp", out_h.ap()[t0 + GRP * g:t0 + GRP * (g + 1), :], fsl, reads=[cb])

    if dbg:
        dbg_specs = {
            "xn": (xn_d, [128, D]), "xq": (xq_d, [128, 512]),
            "up": (up_d, [128, 512]), "gt": (gt_d, [128, 512]),
            "gi2": (gi2_d, [128, 512]), "oi": (oi_d, [128, 512]),
            "gf": (gf_d, [128, 512]),
            "sfm": (s_d, [128, 512]), "kf": (kb_d, [128, 512]),
            "vf": (vb_d, [128, 512]), "ot": (ot_d, [128, 512]),
        }
        for name, (src, shp) in dbg_specs.items():
            hh = nc.dram_tensor("dbg_" + name, shp, F32, kind="ExternalOutput")
            p.dma("dp", hh.ap(), src.ap()[0:shp[0], 0:shp[1]])
        hs = nc.dram_tensor("dbg_scal", [1, 32], F32, kind="ExternalOutput")
        p.dma("dp", hs.ap(), scal[:], reads=[B("scal")])
        hc = nc.dram_tensor("dbg_cols", [128, 256], F32, kind="ExternalOutput")
        p.dma("dp", hc.ap(), colsA[:], reads=[B("m_cab")])
        hst = nc.dram_tensor("dbg_stm", [128, 128], F32, kind="ExternalOutput")
        p.dma("dp", hst.ap(), striptm[:], reads=[B("m_stm")])

    p.wait("v", p.qtok("dp"))
    p.op("v", lambda v: v.engine_nop())

    p.build(es)
    es.close()
    return nc


_PASS_A_ARGS = set("""nc p B Lc TT NT NG dbg mm pick ts tt stt act vtrans copy memset reduce
x_h w_attn conv_h wb_h wg_h wh_h bg_h bh_h nw_h consts xn_d s_d kb_d vb_d ot_d
mskD mskL msku eye_s idn_s onesr hsel_s cw_s nwbc nwrow biasc epsc scal scalb rowws
Sf Sbf warena woarena xstage xnfm bstg1 bstg2 bstg3 fstg1 prear qkvfm chain solv
rtmp colsA strips strips2 wsm striptm mcol mstash PSA PSB PSK PSR PSW PSO PSX PSS""".split())


def _emit_pass_a(**env):
    g = env
    nc, p, B = g["nc"], g["p"], g["B"]
    dbg = g.get("dbg", False)

    _taps = {}

    def tap(name, sl, shape, rd):
        if not dbg:
            return
        if name not in _taps:
            _taps[name] = nc.dram_tensor("tap_" + name, shape, F32, kind="ExternalOutput")
        p.dma("dp", _taps[name].ap(), sl, reads=rd)
    Lc, TT, NT, NG = g["Lc"], g["TT"], g["NT"], g["NG"]
    mm, pick, ts, tt, stt = g["mm"], g["pick"], g["ts"], g["tt"], g["stt"]
    act, vtrans, copy, memset, reduce = g["act"], g["vtrans"], g["copy"], g["memset"], g["reduce"]
    w_attn, conv_h = g["w_attn"], g["conv_h"]
    wb_h, wg_h, wh_h = g["wb_h"], g["wg_h"], g["wh_h"]
    xn_d, s_d, kb_d, vb_d, ot_d = g["xn_d"], g["s_d"], g["kb_d"], g["vb_d"], g["ot_d"]
    mskD, mskL, msku, eye_s, idn_s = g["mskD"], g["mskL"], g["msku"], g["eye_s"], g["idn_s"]
    hsel_s, cw_s, nwbc, biasc, epsc = g["hsel_s"], g["cw_s"], g["nwbc"], g["biasc"], g["epsc"]
    Sf, Sbf, warena, woarena = g["Sf"], g["Sbf"], g["warena"], g["woarena"]
    xstage, xnfm = g["xstage"], g["xnfm"]
    bstg1, bstg2, bstg3, fstg1 = g["bstg1"], g["bstg2"], g["bstg3"], g["fstg1"]
    prear, qkvfm, chain, solv = g["prear"], g["qkvfm"], g["chain"], g["solv"]
    rtmp, colsA, strips, strips2 = g["rtmp"], g["colsA"], g["strips"], g["strips2"]
    wsm, striptm, mstash = g["wsm"], g["striptm"], g["mstash"]
    PSA, PSB, PSK, PSR, PSW, PSO, PSX, PSS = (g["PSA"], g["PSB"], g["PSK"], g["PSR"],
                                              g["PSW"], g["PSO"], g["PSX"], g["PSS"])

    cD, cB, cB2, cD2, cB4, cD4, cB8, cD8, cB16 = range(9)
    cC1, cC1t, cC2, cC2t, cC4, cC4t, cC8, cC8t, cCC, cKKL, cAT = range(9, 20)

    def ch(i):
        return chain[:, 128 * i:128 * (i + 1)]

    def chb(i):
        return B("m_ch%d" % i)

    def sv(i):
        return solv[:, 256 * i:256 * (i + 1)]

    def svb(i):
        return B("m_sv%d" % i)

    for wi, wname in enumerate(("Wq", "Wk", "Wv")):
        for k in range(8):
            p.dma("ds", xstage[:, 0:1024], w_attn[wname].ap()[128 * k:128 * (k + 1), :],
                  reads=[B("xstage")], writes=[B("xstage")])
            copy(pick(("v", "p")), warena[:, 8192 * wi + 1024 * k:8192 * wi + 1024 * (k + 1)],
                 xstage[:, 0:1024], reads=[B("xstage")], writes=[B("warena")])
    for k in range(8):
        p.dma("ds", xstage[:, 0:1024], w_attn["Wo"].ap()[128 * k:128 * (k + 1), :],
              reads=[B("xstage")], writes=[B("xstage")])
        copy(pick(("v", "p")), woarena[:, 1024 * k:1024 * (k + 1)], xstage[:, 0:1024],
             reads=[B("xstage")], writes=[B("woarena")])
    p.dma("ds", xstage[:, 0:64], bass.AP(wb_h, 0, [[8, 128], [1024, 8], [1, 8]]),
          reads=[B("xstage")], writes=[B("xstage")])
    p.dma("ds", xstage[:, 64:96], bass.AP(wg_h, 0, [[4, 128], [512, 8], [1, 4]]),
          reads=[B("xstage")], writes=[B("xstage")])
    p.dma("ds", xstage[:, 96:128], bass.AP(wh_h, 0, [[4, 128], [512, 8], [1, 4]]),
          reads=[B("xstage")], writes=[B("xstage")])
    copy("v", bass.AP(wsm, 0, [[128, 128], [16, 8], [1, 8]]),
         bass.AP(xstage, 0, [[3072, 128], [8, 8], [1, 8]]),
         reads=[B("xstage")], writes=[B("wsm")])
    copy("v", bass.AP(wsm, 8, [[128, 128], [16, 8], [1, 4]]),
         bass.AP(xstage, 64, [[3072, 128], [4, 8], [1, 4]]),
         reads=[B("xstage")], writes=[B("wsm")])
    copy("v", bass.AP(wsm, 12, [[128, 128], [16, 8], [1, 4]]),
         bass.AP(xstage, 96, [[3072, 128], [4, 8], [1, 4]]),
         reads=[B("xstage")], writes=[B("wsm")])
    memset("v", strips[:], 0.0, writes=[B("m_strip")])
    memset("v", mstash[:], 0.0, writes=[B("m_mst")])

    def kfm(h, c, gi):
        return qkvfm[:, 4096 + (2 * h + c) * 512 + gi * 128:
                     4096 + (2 * h + c) * 512 + gi * 128 + 128]

    def qfm(h, c, gi):
        return qkvfm[:, (2 * h + c) * 512 + gi * 128:(2 * h + c) * 512 + gi * 128 + 128]

    def ktm_sl(h, c, gi):
        return bstg2[:, gi * 1024 + h * 256 + c * 128:gi * 1024 + h * 256 + c * 128 + 128]

    def vtm_sl(h, gi):
        return bstg3[:, gi * 1024 + h * 256:gi * 1024 + h * 256 + 256]

    def col(gi, off, h):
        return colsA[:, gi * 64 + off + h:gi * 64 + off + h + 1]

    def Ssl(s, h, c):
        i = (s * 4 + h) * 2 + c
        return Sbf[:, 256 * i:256 * (i + 1)]

    def Sfsl(s, h):
        i = (s * 4 + h) * 2
        return Sf[:, 256 * i:256 * (i + 2)]

    def Sb_buf(s, h):
        return B("m_Sb%d" % (s * 4 + h))

    def Sf_buf(s, h):
        return B("m_Sf%d" % (s * 4 + h))

    RQ, RK, BL, BG, WL, WG, WR, BPL, BPG, RBL, RBG, RK2, SCR = \
        0, 4, 8, 12, 16, 20, 24, 28, 32, 36, 40, 44, 48

    for it in range(NT):
        t0 = it * TT
        for k in range(8):
            p.dma("da", xnfm[:, TT * k:TT * (k + 1)],
                  xn_d.ap()[t0:t0 + TT, 128 * k:128 * (k + 1)], transpose=True,
                  reads=[B("xnfm")], writes=[B("xnfm")])
        for wi in range(3):
            for m in range(8):
                bank = PSA if (m % 2 == 0) else PSB
                bb = B("psAB%d" % (m % 2))
                for k in range(8):
                    mm(bank[:, 0:TT], warena[:, 8192 * wi + 1024 * k + 128 * m:
                                             8192 * wi + 1024 * k + 128 * (m + 1)],
                       xnfm[:, TT * k:TT * (k + 1)], k == 0, k == 7,
                       [B("warena"), B("xnfm")], [bb])
                slot = wi * 8 + m
                base = slot * 516
                act(prear[:, base + 4:base + 4 + TT], bank[:, 0:TT], AF.Copy,
                    reads=[bb], writes=[B("m_pre%d" % slot)])
        for k in range(8):
            mm(PSX[0:16, 0:TT], wsm[:, 16 * k:16 * (k + 1)], xnfm[:, TT * k:TT * (k + 1)],
               k == 0, k == 7, [B("wsm"), B("xnfm")], [B("m_psX")])
        act(strips[0:16, 0:TT], PSX[0:16, 0:TT], AF.Identity, bias=biasc[0:16, 0:1],
            reads=[B("m_psX"), B("biasc")], writes=[B("m_strip")])

        for wi in range(3):
            for m in range(8):
                slot = wi * 8 + m
                base = slot * 516
                pre = B("m_pre%d" % slot)
                acc = xstage[:, 512 * (m % 2):512 * (m % 2) + TT]
                accb = B("m_cacc%d" % (m % 2))
                copy(pick(("v", "p")), prear[:, base + 1:base + 4],
                     mstash[:, 3 * slot:3 * slot + 3],
                     reads=[B("m_mst")], writes=[pre])
                for j in range(4):
                    cwc = cw_s[:, 32 * wi + 4 * m + j:32 * wi + 4 * m + j + 1]
                    src = prear[:, base + 1 + j:base + 1 + j + TT]
                    if j == 0:
                        ts(pick(("v", "p")), acc, src, cwc, None, ALU.mult, None,
                           reads=[pre, B("cw_s")], writes=[accb])
                    else:
                        stt(pick(("v", "p")), acc, src, cwc, acc, ALU.mult, ALU.add,
                            reads=[pre, B("cw_s"), accb], writes=[accb])
                if wi == 0:
                    db = B("m_qf%d" % m)
                    dst = qkvfm[:, m * 512:(m + 1) * 512]
                elif wi == 1:
                    db = B("m_kf%d" % m)
                    dst = qkvfm[:, 4096 + m * 512:4096 + (m + 1) * 512]
                else:
                    db = B("m_vf%d" % (m % 2))
                    dst = bstg1[:, 2048 + 512 * (m % 2):2048 + 512 * (m % 2) + TT]
                act(dst, acc, AF.Silu, reads=[accb], writes=[db])
                if wi == 1:
                    p.dma("dp", kb_d.ap()[128 * m:128 * (m + 1), 0:TT], dst,
                          reads=[db], writes=[B("d_kb%d" % m)])
                if wi == 2:
                    p.dma("dp", vb_d.ap()[128 * m:128 * (m + 1), 0:TT], dst,
                          reads=[db], writes=[B("d_vb%d" % m)])
                copy(pick(("v", "p")), mstash[:, 3 * slot:3 * slot + 3],
                     prear[:, base + 1 + TT:base + 4 + TT],
                     reads=[pre], writes=[B("m_mst")])

        for qk, regb in ((0, "m_qf"), (1, "m_kf")):
            for m in range(8):
                srcsl = qkvfm[:, 4096 * qk + m * 512:4096 * qk + (m + 1) * 512]
                sq = chain[:, 512 * (m % 2):512 * (m % 2) + TT]
                sqb = B("m_chsq%d" % (m % 2))
                srcb = B(("m_qf%d" if qk == 0 else "m_kf%d") % m)
                act(sq, srcsl, AF.Square, reads=[srcb], writes=[sqb])
                bank = PSX if qk == 0 else PSW
                bkb = B("m_psX") if qk == 0 else B("m_psW")
                mm(bank[0:4, 0:TT], hsel_s[:, 4 * (m // 2):4 * (m // 2) + 4], sq,
                   m == 0, m == 7, [B("hsel_s"), sqb], [bkb])
            act(strips2[0:4, 512 * qk:512 * qk + TT],
                (PSX if qk == 0 else PSW)[0:4, 0:TT], AF.Copy,
                reads=[B("m_psX") if qk == 0 else B("m_psW")], writes=[B("m_strip2")])
        for gi in range(NG):
            p.op("t", lambda t, gi=gi: t.transpose(
                PSR[:, 256 + 32 * 0:256 + 32 * 0 + 16], strips[0:16, 128 * gi:128 * (gi + 1)],
                idn_s[0:16, 0:16]), reads=[B("m_strip"), B("idn_s")], writes=[B("m_psR2")])
            p.op("t", lambda t, gi=gi: t.transpose(
                PSR[:, 256 + 16:256 + 20], strips2[0:4, 128 * gi:128 * (gi + 1)],
                idn_s[0:4, 0:4]), reads=[B("m_strip2"), B("idn_s")], writes=[B("m_psR2")])
            p.op("t", lambda t, gi=gi: t.transpose(
                PSR[:, 256 + 20:256 + 24], strips2[0:4, 512 + 128 * gi:512 + 128 * (gi + 1)],
                idn_s[0:4, 0:4]), reads=[B("m_strip2"), B("idn_s")], writes=[B("m_psR2")])
            copy("v", striptm[:, 32 * gi:32 * gi + 24], PSR[:, 256:280],
                 reads=[B("m_psR2")], writes=[B("m_stm")])

        def sAP(off, n):
            return bass.AP(striptm, off, [[128, 128], [32, 4], [1, n]])

        def cAP(off, n):
            return bass.AP(colsA, off, [[256, 128], [64, 4], [1, n]])

        stm, cab = B("m_stm"), B("m_cab")
        act(cAP(BL, 8), sAP(0, 8), AF.Sigmoid, reads=[stm], writes=[cab])
        act(cAP(SCR, 8), sAP(16, 8), AF.Sqrt, bias=epsc[:, 1:2],
            reads=[stm, B("epsc")], writes=[cab])
        p.op("v", lambda v: v.reciprocal(cAP(RQ, 8), cAP(SCR, 8)),
             reads=[cab], writes=[cab])
        tt("v", cAP(SCR, 4), sAP(8, 4), sAP(12, 4), ALU.max, reads=[stm], writes=[cab])
        ts("v", cAP(SCR, 4), cAP(SCR, 4), 0.0, None, ALU.max, None, reads=[cab], writes=[cab])
        tt("v", cAP(SCR + 4, 4), sAP(8, 4), cAP(SCR, 4), ALU.subtract,
           reads=[stm, cab], writes=[cab])
        act(cAP(SCR + 4, 4), cAP(SCR + 4, 4), AF.Exp, reads=[cab], writes=[cab])
        tt("v", cAP(SCR + 8, 4), sAP(12, 4), cAP(SCR, 4), ALU.subtract,
           reads=[stm, cab], writes=[cab])
        act(cAP(SCR + 8, 4), cAP(SCR + 8, 4), AF.Exp, reads=[cab], writes=[cab])
        act(cAP(SCR + 12, 4), cAP(SCR, 4), AF.Exp, scale=-1.0, reads=[cab], writes=[cab])
        tt("v", cAP(SCR, 4), cAP(SCR + 4, 4), cAP(SCR + 8, 4), ALU.add, reads=[cab], writes=[cab])
        tt("v", cAP(SCR, 4), cAP(SCR, 4), cAP(SCR + 12, 4), ALU.add, reads=[cab], writes=[cab])
        p.op("v", lambda v: v.reciprocal(cAP(SCR, 4), cAP(SCR, 4)), reads=[cab], writes=[cab])
        tt("v", cAP(WL, 4), cAP(SCR + 4, 4), cAP(SCR, 4), ALU.mult, reads=[cab], writes=[cab])
        tt("v", cAP(WG, 4), cAP(SCR + 8, 4), cAP(SCR, 4), ALU.mult, reads=[cab], writes=[cab])
        tt("v", cAP(WR, 4), cAP(SCR + 12, 4), cAP(SCR, 4), ALU.mult, reads=[cab], writes=[cab])
        tt("v", cAP(RK2, 4), cAP(RK, 4), cAP(RK, 4), ALU.mult, reads=[cab], writes=[cab])
        tt("v", cAP(BPL, 4), cAP(BL, 4), cAP(RK2, 4), ALU.mult, reads=[cab], writes=[cab])
        tt("v", cAP(BPG, 4), cAP(BG, 4), cAP(RK2, 4), ALU.mult, reads=[cab], writes=[cab])
        tt("v", cAP(RBL, 4), cAP(BL, 4), cAP(RK, 4), ALU.mult, reads=[cab], writes=[cab])
        tt("v", cAP(RBG, 4), cAP(BG, 4), cAP(RK, 4), ALU.mult, reads=[cab], writes=[cab])

        for gi in range(NG):
            p.dma("da", bstg2[:, gi * 1024:(gi + 1) * 1024],
                  kb_d.ap()[:, GRP * gi:GRP * (gi + 1)], transpose=True,
                  reads=[B("m_kt%d" % gi)] + [B("d_kb%d" % m) for m in range(8)],
                  writes=[B("m_kt%d" % gi)])
            p.dma("da", bstg3[:, gi * 1024:(gi + 1) * 1024],
                  vb_d.ap()[:, GRP * gi:GRP * (gi + 1)], transpose=True,
                  reads=[B("m_vt%d" % gi)] + [B("d_vb%d" % m) for m in range(8)],
                  writes=[B("m_vt%d" % gi)])

        for gi in range(NG):
            ktb, vtb = B("m_kt%d" % gi), B("m_vt%d" % gi)
            for h in range(4):
                qfb = [B("m_qf%d" % (2 * h + c)) for c in range(2)]
                kfb = [B("m_kf%d" % (2 * h + c)) for c in range(2)]
                for c in range(2):
                    mm(PSK[:, 0:128], kfm(h, c, gi), kfm(h, c, gi), c == 0, c == 1,
                       [kfb[c]], [B("m_psK")])
                for c in range(2):
                    mm(PSK[:, 128:256], kfm(h, c, gi), qfm(h, c, gi), c == 0, c == 1,
                       [kfb[c], qfb[c]], [B("m_psKq")])
                tt("v", ch(cKKL), PSK[:, 0:128], mskL[:], ALU.mult,
                   reads=[B("m_psK"), B("mskL")], writes=[chb(cKKL)])
                tt("v", ch(cAT), PSK[:, 128:256], msku[:], ALU.mult,
                   reads=[B("m_psKq"), B("msku")], writes=[chb(cAT)])
                if it == 0 and gi == 0 and h == 0:
                    tap("KKL", ch(cKKL), [128, 128], [chb(cKKL)])
                    tap("AT", ch(cAT), [128, 128], [chb(cAT)])
                for s in range(2):
                    bp = col(gi, BPL if s == 0 else BPG, h)
                    rb = col(gi, RBL if s == 0 else RBG, h)
                    wc = col(gi, WL if s == 0 else WG, h)
                    stt("v", ch(cD), PSK[:, 0:128], bp, mskD[:], ALU.mult, ALU.mult,
                        reads=[B("m_psK"), B("m_cab"), B("mskD")], writes=[chb(cD)])
                    vtrans(ch(cB), ch(cD), reads=[chb(cD)], writes=[chb(cB)])
                    for (src, dst, tr) in ((cB, cB2, cD2), (cB2, cB4, cD4),
                                           (cB4, cB8, cD8), (cB8, cB16, None)):
                        lhs = {cB: cD, cB2: cD2, cB4: cD4, cB8: cD8}[src]
                        mm(PSK[:, 256:384], ch(lhs), ch(src), True, True,
                           [chb(lhs), chb(src)], [B("m_psC")])
                        act(ch(dst), PSK[:, 256:384], AF.Copy,
                            reads=[B("m_psC")], writes=[chb(dst)])
                        if tr is not None:
                            vtrans(ch(tr), ch(dst), reads=[chb(dst)], writes=[chb(tr)])
                    tt("p", ch(cC1), ch(cB), eye_s[:], ALU.add,
                       reads=[chb(cB), B("eye_s")], writes=[chb(cC1)])
                    tt("p", ch(cC1t), ch(cD), eye_s[:], ALU.add,
                       reads=[chb(cD), B("eye_s")], writes=[chb(cC1t)])
                    for (lt, bpow, dst, tr) in ((cC1t, cB2, cC2, cC2t),
                                                (cC2t, cB4, cC4, cC4t),
                                                (cC4t, cB8, cC8, cC8t),
                                                (cC8t, cB16, cCC, None)):
                        prevc = {cC2: cC1, cC4: cC2, cC8: cC4, cCC: cC8}[dst]
                        mm(PSK[:, 256:384], ch(lt), ch(bpow), True, True,
                           [chb(lt), chb(bpow)], [B("m_psC")])
                        tt("v", ch(dst), PSK[:, 256:384], ch(prevc), ALU.add,
                           reads=[B("m_psC"), chb(prevc)], writes=[chb(dst)])
                        if tr is not None:
                            vtrans(ch(tr), ch(dst), reads=[chb(dst)], writes=[chb(tr)])
                    if it == 0 and gi == 0 and h == 0 and s == 0:
                        tap("D", ch(cD), [128, 128], [chb(cD)])
                        tap("CC", ch(cCC), [128, 128], [chb(cCC)])
                    for c in range(2):
                        mm(PSR[:, 0:256], kfm(h, c, gi), Ssl(s, h, c), c == 0, c == 1,
                           [kfb[c], Sb_buf(s, h)], [B("m_psR")])
                    act(rtmp[:, 128:384], PSR[:, 0:256], AF.Copy, scale=bp,
                        reads=[B("m_psR"), B("m_cab")], writes=[B("m_rt")])
                    stt("v", sv(0), vtm_sl(h, gi), rb, rtmp[:, 128:384],
                        ALU.mult, ALU.subtract,
                        reads=[vtb, B("m_cab"), B("m_rt")], writes=[svb(0)])
                    mm(PSW[:, 0:256], ch(cCC), sv(0), True, True,
                       [chb(cCC), svb(0)], [B("m_psW")])
                    act(sv(1), PSW[:, 0:256], AF.Copy, scale=wc,
                        reads=[B("m_psW"), B("m_cab")], writes=[svb(1)])
                    cur = 1
                    for lev in range(3):
                        mm(PSR[:, 0:256], ch(cKKL), sv(cur), True, True,
                           [chb(cKKL), svb(cur)], [B("m_psR")])
                        act(sv(3), PSR[:, 0:256], AF.Copy, scale=bp,
                            reads=[B("m_psR"), B("m_cab")], writes=[svb(3)])
                        pw = PSW[:, 256:512] if lev % 2 == 0 else PSW[:, 0:256]
                        pwb = B("m_psW2") if lev % 2 == 0 else B("m_psW")
                        mm(pw, ch(cCC), sv(3), True, False,
                           [chb(cCC), svb(3)], [pwb])
                        mm(pw, ch(cCC), sv(0), False, True,
                           [chb(cCC), svb(0)], [pwb])
                        dst_s = (4 + s) if lev == 2 else (2 if cur == 1 else 1)
                        act(sv(dst_s), pw, AF.Copy, scale=wc,
                            reads=[pwb, B("m_cab")], writes=[svb(dst_s)])
                        cur = dst_s
                    if it == 0 and gi == 0 and h == 0 and s == 0:
                        tap("rhat", sv(0), [128, 256], [svb(0)])
                        tap("t0w", sv(1), [128, 256], [svb(1)])
                        tap("upp", sv(4 + s), [128, 256], [svb(4 + s)])
                    for c in range(2):
                        mm(PSO[:, 0:256], qfm(h, c, gi), Ssl(s, h, c),
                           (s == 0 and c == 0), False,
                           [qfb[c], Sb_buf(s, h)], [B("m_psO")])
                    mm(PSO[:, 0:256], ch(cAT), sv(4 + s), False, s == 1,
                       [chb(cAT), svb(4 + s)], [B("m_psO")])
                    for c in range(2):
                        mm(PSS[:, 256 * c:256 * (c + 1)], ktm_sl(h, c, gi), sv(4 + s),
                           True, True, [ktb, svb(4 + s)], [B("m_psS")])
                    tt("v", Sfsl(s, h), Sfsl(s, h), PSS[:, 0:512], ALU.add,
                       reads=[B("m_psS")], writes=[Sf_buf(s, h)])
                    copy("p", Sbf[:, 512 * (s * 4 + h):512 * (s * 4 + h + 1)], Sfsl(s, h),
                         reads=[Sf_buf(s, h)], writes=[Sb_buf(s, h)])
                if it == 0 and gi == 0 and h == 0:
                    tap("S0", Sbf[:, 0:512], [128, 512], [Sb_buf(0, 0)])
                otsl = fstg1[:, (gi % 2) * 1024 + h * 256:(gi % 2) * 1024 + h * 256 + 256]
                otb = B("m_otm%d_%d" % (gi % 2, h))
                act(rtmp[:, 128:384], PSO[:, 0:256], AF.Copy, scale=col(gi, RQ, h),
                    reads=[B("m_psO"), B("m_cab")], writes=[B("m_rt")])
                stt("v", otsl, vtm_sl(h, gi), col(gi, WR, h), rtmp[:, 128:384],
                    ALU.mult, ALU.add,
                    reads=[vtb, B("m_cab"), B("m_rt")], writes=[otb])
                act(rtmp[:, 128:384], otsl, AF.Square,
                    reads=[otb], writes=[B("m_rt")], accum=rtmp[:, 440 + h:441 + h])
                act(rtmp[:, 444 + h:445 + h], rtmp[:, 440 + h:441 + h], AF.Sqrt,
                    scale=1.0 / DV, bias=epsc[:, 0:1],
                    reads=[B("m_rt"), B("epsc")], writes=[B("m_rs%d" % h)])
                p.op("v", lambda v, h=h: v.reciprocal(rtmp[:, 448 + h:449 + h],
                                                      rtmp[:, 444 + h:445 + h]),
                     reads=[B("m_rs%d" % h)], writes=[B("m_rs%d" % h)])
                onsl = bstg1[:, (gi % 2) * 1024 + h * 256:(gi % 2) * 1024 + h * 256 + 256]
                stt("v", onsl, otsl, rtmp[:, 448 + h:449 + h], nwbc[:],
                    ALU.mult, ALU.mult,
                    reads=[otb, B("m_rs%d" % h), B("nwbc")],
                    writes=[B("m_on%d_%d" % (gi % 2, h))])
            if it == 0 and gi == 0:
                tap("otm", fstg1[:, 0:256], [128, 256], [B("m_otm0_0")])
                tap("qf0", qkvfm[:, 0:512], [128, 512], [B("m_qf0")])
                tap("kf0", qkvfm[:, 4096:4608], [128, 512], [B("m_kf0")])
                tap("kt0", bstg2[:, 0:1024], [128, 1024], [B("m_kt0")])
                tap("vt0", bstg3[:, 0:1024], [128, 1024], [B("m_vt0")])
            p.dma("dp", ot_d.ap()[GRP * gi:GRP * (gi + 1), :],
                  bstg1[:, (gi % 2) * 1024:(gi % 2) * 1024 + 1024],
                  reads=[B("m_on%d_%d" % (gi % 2, h2)) for h2 in range(4)],
                  writes=[B("d_ot%d" % gi)])

        for k in range(8):
            p.dma("da", xnfm[:, TT * k:TT * (k + 1)],
                  ot_d.ap()[0:TT, 128 * k:128 * (k + 1)], transpose=True,
                  reads=[B("xnfm")] + [B("d_ot%d" % g2) for g2 in range(NG)],
                  writes=[B("xnfm")])
        for m in range(8):
            bank = PSA if (m % 2 == 0) else PSB
            bb = B("psAB%d" % (m % 2))
            for k in range(8):
                mm(bank[:, 0:TT], woarena[:, 1024 * k + 128 * m:1024 * k + 128 * (m + 1)],
                   xnfm[:, TT * k:TT * (k + 1)], k == 0, k == 7,
                   [B("woarena"), B("xnfm")], [bb])
            ssl = bstg1[:, 2048 + 512 * (m % 2):2048 + 512 * (m % 2) + TT]
            sb_ = B("m_b1v%d" % (m % 2))
            act(ssl, bank[:, 0:TT], AF.Copy, reads=[bb], writes=[sb_])
            p.dma("dp", s_d.ap()[128 * m:128 * (m + 1), t0:t0 + TT], ssl, reads=[sb_])


def kernel(dbg=False, **inputs):
    x = np.ascontiguousarray(np.asarray(inputs["L0_attn_hidden_states"], dtype=np.float32))
    ncores, Lc = x.shape[0], x.shape[1]
    nc = build_core_program(Lc, 512, with_attn=True, dbg=dbg)
    cmn = {
        "Wq": np.asarray(inputs["L0_attn_Wq"], np.float32),
        "Wk": np.asarray(inputs["L0_attn_Wk"], np.float32),
        "Wv": np.asarray(inputs["L0_attn_Wv"], np.float32),
        "Wo": np.asarray(inputs["L0_attn_Wo"], np.float32),
        "conv_q": np.asarray(inputs["L0_attn_conv_q"], np.float32),
        "conv_k": np.asarray(inputs["L0_attn_conv_k"], np.float32),
        "conv_v": np.asarray(inputs["L0_attn_conv_v"], np.float32),
        "Wb": np.asarray(inputs["L0_attn_Wb"], np.float32),
        "Wg": np.asarray(inputs["L0_attn_Wg"], np.float32),
        "Wh": np.asarray(inputs["L0_attn_Wh"], np.float32),
        "bg": np.asarray(inputs["L0_attn_bg"], np.float32),
        "bh": np.asarray(inputs["L0_attn_bh"], np.float32),
        "norm_w": np.asarray(inputs["L0_attn_norm_w"], np.float32),
        "w_up": np.asarray(inputs["L0_ffn_w_up"], np.float32),
        "w_gate": np.asarray(inputs["L0_ffn_w_gate"], np.float32),
        "w_down": np.asarray(inputs["L0_ffn_w_down"], np.float32),
    }
    cmn.update(host_constants())
    in_maps = [dict(cmn, x=np.ascontiguousarray(x[b])) for b in range(ncores)]
    res = run_bass_kernel_spmd(nc, in_maps, core_ids=list(range(ncores)))
    if dbg:
        return res.results
    return np.stack([r["out"] for r in res.results], axis=0)
